# version 03
# baseline (speedup 1.0000x reference)
import os
import numpy as np
import ml_dtypes
import concourse.bass as bass
import concourse.mybir as mybir
from concourse.bass_utils import run_bass_kernel_spmd
from contextlib import ExitStack

F32 = mybir.dt.float32
BF16 = mybir.dt.bfloat16
I32 = mybir.dt.int32
AF = mybir.ActivationFunctionType
ALU = mybir.AluOpType
AX = mybir.AxisListType.X

P = 128
D = 1024
DG = D // P
H = 4096
HG = H // P
TT = 512
NSUB = TT // P
HT = 256
FS = 2 * HT

TH_COMP = [0.5 + 2**-9, 1.5 - 2**-8, 2.5 + 2**-7, 3.5 - 2**-7,
           4.5 + 2**-6, 5.5 - 2**-6, 6.5 + 2**-6, 7.5 - 2**-6]
TH_PLAIN = [0.5, 1.5, 2.5, 3.5, 4.5, 5.5, 6.5, 7.5]

SAFE = os.environ.get("SAFE", "0") == "1"


class K:
    def __init__(self, nc, es):
        self.nc = nc
        self.sem = {}
        self.cnt = {}
        for name in ("ld", "gq", "te", "act", "dve"):
            self.sem[name] = es.enter_context(nc.semaphore(name))
            self.cnt[name] = 0

    def inc(self, instr, name, amt=1):
        instr.then_inc(self.sem[name], amt)
        self.cnt[name] += amt
        return instr

    def wait(self, eng, name):
        if SAFE:
            for n in ("ld", "gq", "te", "act", "dve"):
                eng.wait_ge(self.sem[n], self.cnt[n])
        else:
            eng.wait_ge(self.sem[name], self.cnt[name])

    def wait_all(self, eng):
        for n in ("te", "act", "dve"):
            self.wait(eng, n)


def emit_ln(k, act, dve, x_ap, xhat_ap, stats, eps5, sq):
    k.inc(dve.reduce_sum(out=stats[:, 0:1], in_=x_ap, axis=AX), "dve")
    k.wait(act, "dve")
    k.inc(act.mul(stats[:, 1:2], stats[:, 0:1], -1.0 / D), "act")
    k.wait(dve, "act")
    dve.tensor_scalar(out=sq, in0=x_ap, scalar1=stats[:, 1:2], scalar2=None, op0=ALU.add)
    dve.tensor_mul(out=sq, in0=sq, in1=sq)
    k.inc(dve.reduce_sum(out=stats[:, 2:3], in_=sq, axis=AX), "dve")
    k.wait(act, "dve")
    act.activation(stats[:, 3:4], stats[:, 2:3], AF.Ln, bias=eps5[:, 0:1], scale=1.0 / D)
    k.inc(act.activation(stats[:, 3:4], stats[:, 3:4], AF.Exp, bias=0.0, scale=-0.5), "act")
    k.wait(dve, "act")
    k.inc(dve.tensor_scalar(out=xhat_ap, in0=x_ap, scalar1=stats[:, 1:2],
                            scalar2=stats[:, 3:4], op0=ALU.add, op1=ALU.mult), "dve")


def emit_split(k, dve, src, hi, lo):
    dve.tensor_copy(out=hi, in_=src)
    k.inc(dve.tensor_tensor(out=lo, in0=src, in1=hi, op=ALU.subtract), "dve")


def emit_fq_scale(dve, src_pos, sh, s2b):
    dve.tensor_single_scalar(out=sh, in_=src_pos.bitcast(I32), scalar=23,
                             op=ALU.logical_shift_right)
    dve.tensor_scalar(out=sh, in0=sh, scalar1=113, scalar2=2, op0=ALU.max, op1=ALU.subtract)
    dve.tensor_single_scalar(out=s2b, in_=sh, scalar=23, op=ALU.logical_shift_left)
    dve.tensor_scalar(out=sh, in0=s2b, scalar1=-1, scalar2=254 << 23,
                      op0=ALU.mult, op1=ALU.add)


def emit_staircase(dve, t2, n2, tmp, ths):
    dve.tensor_single_scalar(out=n2, in_=t2, scalar=float(ths[0]), op=ALU.is_ge)
    cur, other = n2, tmp
    for th in ths[1:]:
        dve.scalar_tensor_tensor(out=other, in0=t2, scalar=float(th), in1=cur,
                                 op0=ALU.is_ge, op1=ALU.add)
        cur, other = other, cur
    return cur


def emit_fq_signed(k, act, dve, src, dst_bf16, za, sg, sh, s2b, t2, n2, tmp):
    k.wait(act, "dve")
    act.activation(za, src, AF.Abs, bias=0.0, scale=1.0)
    k.inc(act.activation(sg, src, AF.Sign, bias=0.0, scale=1.0), "act")
    k.wait(dve, "act")
    emit_fq_scale(dve, za, sh, s2b)
    dve.tensor_mul(out=t2, in0=za, in1=sh.bitcast(F32))
    nacc = emit_staircase(dve, t2, n2, tmp, TH_PLAIN)
    dve.tensor_mul(out=za, in0=nacc, in1=s2b.bitcast(F32))
    k.inc(dve.tensor_mul(out=dst_bf16, in0=za, in1=sg), "dve")


def emit_hchain(k, dve, h_pos, dst_bf16, sh, s2b, t2, n2, tmp):
    emit_fq_scale(dve, h_pos, sh, s2b)
    dve.tensor_mul(out=t2, in0=h_pos, in1=sh.bitcast(F32))
    nacc = emit_staircase(dve, t2, n2, tmp, TH_COMP)
    k.inc(dve.tensor_mul(out=dst_bf16, in0=nacc, in1=s2b.bitcast(F32)), "dve")


def emit_rm_signed(k, act, dve, src, dst, za, sg, sh, s1b, t1, vv, gg, tmp):
    k.wait(act, "dve")
    act.activation(za, src, AF.Abs, bias=0.0, scale=1.0)
    k.inc(act.activation(sg, src, AF.Sign, bias=0.0, scale=1.0), "act")
    k.wait(dve, "act")
    dve.tensor_single_scalar(out=sh, in_=za.bitcast(I32), scalar=23, op=ALU.logical_shift_right)
    dve.tensor_scalar(out=sh, in0=sh, scalar1=95, scalar2=7, op0=ALU.max, op1=ALU.subtract)
    dve.tensor_single_scalar(out=s1b, in_=sh, scalar=23, op=ALU.logical_shift_left)
    dve.tensor_scalar(out=sh, in0=s1b, scalar1=-1, scalar2=254 << 23, op0=ALU.mult, op1=ALU.add)
    dve.tensor_mul(out=t1, in0=za, in1=sh.bitcast(F32))
    dve.tensor_single_scalar(out=vv, in_=t1, scalar=128.0, op=ALU.subtract)
    cur, other = vv, gg
    sign = 1
    for b in (64, 32, 16, 8, 4, 2, 1):
        if sign > 0:
            dve.tensor_single_scalar(out=tmp, in_=cur, scalar=float(b), op=ALU.is_ge)
        else:
            dve.tensor_single_scalar(out=tmp, in_=cur, scalar=float(-b), op=ALU.is_le)
        dve.scalar_tensor_tensor(out=other, in0=tmp, scalar=float(sign * b), in1=cur,
                                 op0=ALU.mult, op1=ALU.subtract)
        cur, other = other, cur
        sign = -sign
    if sign > 0:
        dve.tensor_single_scalar(out=tmp, in_=cur, scalar=0.5, op=ALU.is_ge)
    else:
        dve.tensor_single_scalar(out=tmp, in_=cur, scalar=-0.5, op=ALU.is_le)
    dve.scalar_tensor_tensor(out=other, in0=cur, scalar=float(-sign), in1=t1,
                             op0=ALU.mult, op1=ALU.add)
    dve.tensor_add(out=other, in0=other, in1=tmp)
    dve.tensor_mul(out=other, in0=other, in1=s1b.bitcast(F32))
    k.inc(dve.tensor_mul(out=dst, in0=other, in1=sg), "dve")


class Accum:

    def __init__(self, te, out_ap, total):
        self.te, self.out, self.total = te, out_ap, total
        self.n = 0

    def mm(self, lhsT, rhs):
        self.n += 1
        return self.te.matmul(self.out, lhsT, rhs,
                              start=(self.n == 1), stop=(self.n == self.total))

    def mm3(self, lh, ll, rh, rl):
        self.mm(lh, rh)
        self.mm(lh, rl)
        return self.mm(ll, rh)


def build_program(T):
    NT = T // TT
    DBG = os.environ.get("DBG", "0") == "1"
    SCR = "ExternalOutput" if DBG else "Internal"
    nc = bass.Bass("TRN2", target_bir_lowering=False, detect_race_conditions=False)

    x_d = nc.dram_tensor("x", [T, D], F32, kind="ExternalInput").ap()
    mask_d = nc.dram_tensor("mask", [T, D], BF16, kind="ExternalInput").ap()
    w_in = {}
    for nm, shp in (("wq", [D, D]), ("wk", [D, D]), ("wv", [D, D]), ("wa", [D, D]),
                    ("w1", [H, D]), ("w2", [D, H])):
        w_in[nm] = nc.dram_tensor(nm, shp, F32, kind="ExternalInput").ap()
    b_in = {}
    for nm, shp in (("bq", [P, DG]), ("bk", [P, DG]), ("bv", [1, D]), ("ba", [P, DG]),
                    ("b1", [P, HG]), ("b2", [P, DG])):
        b_in[nm] = nc.dram_tensor(nm, shp, F32, kind="ExternalInput").ap()
    ident_d = nc.dram_tensor("ident", [P, P], BF16, kind="ExternalInput").ap()
    trimask_d = nc.dram_tensor("trimask", [P, P], F32, kind="ExternalInput").ap()
    out_d = nc.dram_tensor("out", [T, D], F32, kind="ExternalOutput").ap()

    xhatT_h_s = nc.dram_tensor("xhatT_h_s", [D, T], BF16, kind=SCR).ap()
    xhatT_l_s = nc.dram_tensor("xhatT_l_s", [D, T], BF16, kind=SCR).ap()
    g_s = nc.dram_tensor("g_s", [D, T], F32, kind=SCR).ap()
    wT_s = {}
    for nm in ("wq", "wk", "wv", "wa"):
        wT_s[nm] = (nc.dram_tensor(nm + "Th_s", [D, D], BF16, kind=SCR).ap(),
                    nc.dram_tensor(nm + "Tl_s", [D, D], BF16, kind=SCR).ap())
    w1qT_s = nc.dram_tensor("w1qT_s", [P, HG * D], BF16, kind=SCR).ap()
    w2qT_s = nc.dram_tensor("w2qT_s", [P, DG * HG * P], BF16, kind=SCR).ap()
    if DBG:
        x2_dbg = nc.dram_tensor("x2_dbg", [T, D], F32, kind="ExternalOutput").ap()
        hq_dbg = nc.dram_tensor("hq_dbg", [P, NT * 2 * HG * HT], BF16,
                                kind="ExternalOutput").ap()
        y2_dbg = nc.dram_tensor("y2_dbg", [P, NT * DG * TT], BF16,
                                kind="ExternalOutput").ap()
        f1_dbg = nc.dram_tensor("f1_dbg", [P, NT * 2 * HG * HT], F32,
                                kind="ExternalOutput").ap()
        x2q_dbg = nc.dram_tensor("x2q_dbg", [P, NT * 2 * DG * HT], BF16,
                                 kind="ExternalOutput").ap()
        w1b_dbg = nc.dram_tensor("w1b_dbg", [P, NT * 2 * HG * D], BF16,
                                 kind="ExternalOutput").ap()
        x2q2_dbg = nc.dram_tensor("x2q2_dbg", [P, NT * 2 * DG * HT], BF16,
                                  kind="ExternalOutput").ap()
        ps_dbg = nc.dram_tensor("ps_dbg", [P, HG * HT], F32,
                                kind="ExternalOutput").ap()
        bq_dbg = nc.dram_tensor("bq_dbg", [P, HG + DG], F32,
                                kind="ExternalOutput").ap()
        bqe_dbg = nc.dram_tensor("bqe_dbg", [P, 8 * HG], F32,
                                 kind="ExternalOutput").ap()
        e1_dbg = nc.dram_tensor("e1_dbg", [P, HG + DG], F32,
                                kind="ExternalOutput").ap()

    with ExitStack() as es:
        k = K(nc, es)
        te, act, dve, gp, sp = nc.tensor, nc.scalar, nc.vector, nc.gpsimd, nc.sync

        def sb(name, shape, dt=F32):
            return es.enter_context(nc.sbuf_tensor(name, shape, dt))

        S_h = sb("S_h", [P, DG * D], BF16); S_l = sb("S_l", [P, DG * D], BF16)
        z_f = sb("z_f", [P, DG])
        z_h = sb("z_h", [P, DG], BF16); z_l = sb("z_l", [P, DG], BF16)
        Gc = sb("Gc", [P, DG])
        bq_t = sb("bq_t", [P, DG]); bk_t = sb("bk_t", [P, DG]); ba_t = sb("ba_t", [P, DG])
        b1q = sb("b1q", [P, HG]); b2q = sb("b2q", [P, DG])
        bv_h = sb("bv_h", [1, D], BF16); bv_l = sb("bv_l", [1, D], BF16)
        ones1 = sb("ones1", [1, P], BF16)
        onescol = sb("onescol", [P, 1], BF16)
        ident_t = sb("ident_t", [P, P], BF16)
        trimask_t = sb("trimask_t", [P, P], F32)
        eps5 = sb("eps5", [P, 1]); eps6 = sb("eps6", [P, 1])
        stats = sb("stats", [P, 16])
        recd = sb("recd", [P, 2])

        xt = sb("xt", [P, NSUB * D])
        xh_f = sb("xh_f", [P, D])
        xh_h = sb("xh_h", [P, D], BF16); xh_l = sb("xh_l", [P, D], BF16)
        xhT_h = sb("xhT_h", [P, DG * TT], BF16); xhT_l = sb("xhT_l", [P, DG * TT], BF16)
        q_h = sb("q_h", [P, DG * TT], BF16); q_l = sb("q_l", [P, DG * TT], BF16)
        wk_h = sb("wk_h", [P, DG * TT], BF16); wk_l = sb("wk_l", [P, DG * TT], BF16)
        wkT_h = sb("wkT_h", [P, NSUB * D], BF16); wkT_l = sb("wkT_l", [P, NSUB * D], BF16)
        v_h = sb("v_h", [P, NSUB * D], BF16); v_l = sb("v_l", [P, NSUB * D], BF16)
        scT_h0 = sb("scT_h0", [P, P], BF16); scT_l0 = sb("scT_l0", [P, P], BF16)
        scT_h1 = sb("scT_h1", [P, P], BF16); scT_l1 = sb("scT_l1", [P, P], BF16)
        sc_f = sb("sc_f", [P, P])
        qv_f = sb("qv_f", [P, TT])
        hq_bf = sb("hq_bf", [P, HG * HT], BF16)
        gq_f = hq_bf[:, 0:2 * TT].bitcast(F32)
        AR = sb("AR", [P, 2 * DG * D], BF16)
        wst_h = AR[:, 0:DG * D]
        wst_l = AR[:, DG * D:2 * DG * D]
        x2qT = AR[:, 0:DG * HT]
        w1blk = AR[:, DG * HT:DG * HT + D]
        w1blk2 = AR[:, DG * HT + D:DG * HT + 2 * D]
        w2blk = AR[:, DG * HT + 2 * D:DG * HT + 2 * D + HG * P]
        _om = DG * HT + 2 * D + HG * P
        mask_t = AR[:, _om:_om + D]
        out_t = AR[:, _om + D:_om + 3 * D].bitcast(F32)
        y2_bf = AR[:, _om + 3 * D:_om + 3 * D + DG * TT]

        f1 = sb("f1", [P, 528]); f2 = sb("f2", [P, 528]); f3 = sb("f3", [P, 528])
        f4 = sb("f4", [P, 528]); f5 = sb("f5", [P, 528]); f6 = sb("f6", [P, 528])
        _jv = v_l[:, :].bitcast(I32)
        j1 = _jv[:, 0:528]; j2 = _jv[:, 528:1056]
        e1, e2, e3 = f4, f5, f6

        ps_a = es.enter_context(nc.psum_tensor("ps_a", [P, TT], F32))
        ps_b = es.enter_context(nc.psum_tensor("ps_b", [P, TT], F32))
        ps_v0 = es.enter_context(nc.psum_tensor("ps_v0", [P, TT], F32))
        ps_v1 = es.enter_context(nc.psum_tensor("ps_v1", [P, TT], F32))
        ps_n0 = es.enter_context(nc.psum_tensor("ps_n0", [P, TT], F32))
        ps_n1 = es.enter_context(nc.psum_tensor("ps_n1", [P, TT], F32))
        ps_sc = es.enter_context(nc.psum_tensor("ps_sc", [P, TT], F32))
        ps_t = es.enter_context(nc.psum_tensor("ps_t", [P, D], BF16))

        LD, GQ = k.sem["ld"], k.sem["gq"]

        def ld_dma(out, in_):
            k.cnt["ld"] += 16
            sp.dma_start(out=out, in_=in_).then_inc(LD, 16)

        def st_dma(out, in_):
            k.cnt["gq"] += 16
            gp.dma_start(out=out, in_=in_).then_inc(GQ, 16)

        def bq_epoch(idx):
            if DBG:
                k.wait(gp, "act")
                k.wait(gp, "dve")
                st_dma(bqe_dbg[:, idx * HG:(idx + 1) * HG], b1q[:])

        def jsl(buf, j):
            return buf[:, :].rearrange("p (g t) -> p g t", g=DG)[:, :, j * P:(j + 1) * P]

        for buf, val in ((eps5, 1e-5), (eps6, 1e-6), (Gc, 0.0), (S_h, 0.0), (S_l, 0.0),
                         (z_f, 0.0), (z_h, 0.0), (z_l, 0.0), (ones1, 1.0), (onescol, 1.0)):
            dve.memset(buf[:], val)
        dve.memset(e1[:, HG + DG:TT], 0.0)
        k.cnt["dve"] += 1
        dve.memset(stats[:], 0.0).then_inc(k.sem["dve"], 1)

        ld_dma(ident_t[:], ident_d[:])
        ld_dma(trimask_t[:], trimask_d[:])
        ld_dma(bq_t[:], b_in["bq"][:])
        ld_dma(bk_t[:], b_in["bk"][:])
        ld_dma(ba_t[:], b_in["ba"][:])
        ld_dma(e1[:, 0:HG], b_in["b1"][:])
        ld_dma(e1[:, HG:HG + DG], b_in["b2"][:])
        ld_dma(xh_f[0:1, 0:D], b_in["bv"][:])
        k.wait(dve, "ld")
        k.wait(act, "ld")
        emit_split(k, dve, xh_f[0:1, 0:D], bv_h[:], bv_l[:])
        emit_fq_signed(k, act, dve, e1[:, 0:TT], xh_h[:, 0:TT],
                       e2[:, 0:TT], e3[:, 0:TT], j1[:, 0:TT], j2[:, 0:TT],
                       f2[:, 0:TT], f3[:, 0:TT], f1[:, 0:TT])
        k.inc(dve.tensor_copy(out=b1q[:], in_=xh_h[:, 0:HG]), "dve")
        k.wait(act, "dve")
        k.inc(act.copy(b2q[:], xh_h[:, HG:HG + DG]), "act")
        if DBG:
            k.wait(gp, "act")
            st_dma(e1_dbg[:, :], e1[:, 0:HG + DG])
        bq_epoch(0)

        def transpose_store(src_bf, dst_view, img):
            k.wait(te, "dve")
            k.wait(te, "act")
            ii = None
            for kb in range(DG):
                ii = te.transpose(ps_t[:, kb * P:(kb + 1) * P],
                                  src_bf[:, kb * P:(kb + 1) * P], ident_t[:])
            k.inc(ii, "te")
            k.wait(act, "te")
            k.wait(act, "gq")
            k.inc(act.activation(xhT_h[:, 0:D], ps_t[:], AF.Copy, bias=0.0, scale=1.0), "act")
            k.wait(gp, "act")
            if img:
                st_dma(dst_view, xhT_h[:, 0:D])
            else:
                st_dma(dst_view, xhT_h[:, 0:D].rearrange("x (a n) -> x a n", n=P))

        for nm in ("wq", "wk", "wv", "wa"):
            for nb in range(DG):
                k.wait_all(sp)
                ld_dma(xt[:, 0:D], w_in[nm][nb * P:(nb + 1) * P, :])
                k.wait(dve, "ld")
                emit_split(k, dve, xt[:, 0:D], xh_h[:], xh_l[:])
                transpose_store(xh_h[:], wT_s[nm][0][:, nb * P:(nb + 1) * P]
                                .rearrange("(a x) n -> x a n", x=P), img=False)
                transpose_store(xh_l[:], wT_s[nm][1][:, nb * P:(nb + 1) * P]
                                .rearrange("(a x) n -> x a n", x=P), img=False)
        bq_epoch(1)

        for nb in range(HG):
            k.wait_all(sp)
            ld_dma(xt[:, 0:D], w_in["w1"][nb * P:(nb + 1) * P, :])
            k.wait(dve, "ld")
            k.wait(act, "ld")
            for hh in range(2):
                sl = slice(hh * TT, (hh + 1) * TT)
                emit_fq_signed(k, act, dve, xt[:, sl], xh_h[:, sl],
                               f1[:, 0:TT], e2[:, 0:TT], j1[:, 0:TT], j2[:, 0:TT],
                               f2[:, 0:TT], f3[:, 0:TT], f4[:, 0:TT])
            transpose_store(xh_h[:], w1qT_s[:, nb * D:(nb + 1) * D], img=True)
        bq_epoch(2)

        for mb in range(DG):
            for hh in range(4):
                k.wait_all(sp)
                ld_dma(xt[:, 0:D], w_in["w2"][mb * P:(mb + 1) * P, hh * D:(hh + 1) * D])
                k.wait(dve, "ld")
                k.wait(act, "ld")
                for q2 in range(2):
                    sl = slice(q2 * TT, (q2 + 1) * TT)
                    emit_fq_signed(k, act, dve, xt[:, sl], xh_h[:, sl],
                                   f1[:, 0:TT], e2[:, 0:TT], j1[:, 0:TT], j2[:, 0:TT],
                                   f2[:, 0:TT], f3[:, 0:TT], f4[:, 0:TT])
                ng0 = hh * DG
                transpose_store(xh_h[:],
                                w2qT_s[:, (mb * HG + ng0) * P:(mb * HG + ng0 + DG) * P],
                                img=True)

        bq_epoch(3)
        k.wait_all(sp)
        sp.wait_ge(GQ, k.cnt["gq"])
        for kb in range(DG):
            ld_dma(wst_h[:, kb * D:(kb + 1) * D], wT_s["wa"][0][kb * P:(kb + 1) * P, :])
            ld_dma(wst_l[:, kb * D:(kb + 1) * D], wT_s["wa"][1][kb * P:(kb + 1) * P, :])

        for ti in range(NT - 1, -1, -1):
            base = ti * TT
            k.wait_all(sp)
            sp.wait_ge(GQ, k.cnt["gq"])
            for j in range(NSUB):
                ld_dma(xt[:, j * D:(j + 1) * D], x_d[base + j * P:base + (j + 1) * P, :])
            k.wait(dve, "ld")
            k.wait(act, "ld")
            for j in range(NSUB):
                emit_ln(k, act, dve, xt[:, j * D:(j + 1) * D], xh_f[:], stats, eps5, xh_f[:])
                emit_split(k, dve, xh_f[:], xh_h[:], xh_l[:])
                for src, dstT in ((xh_h, xhT_h), (xh_l, xhT_l)):
                    k.wait(te, "dve")
                    ii = None
                    for g in range(DG):
                        ii = te.transpose(ps_t[:, g * P:(g + 1) * P],
                                          src[:, g * P:(g + 1) * P], ident_t[:])
                    k.inc(ii, "te")
                    k.wait(dve, "te")
                    k.inc(dve.tensor_copy(
                        out=jsl(dstT, j),
                        in_=ps_t[:, :].rearrange("p (g t) -> p g t", g=DG)), "dve")
            k.wait(gp, "dve")
            for g in range(DG):
                st_dma(xhatT_h_s[g * P:(g + 1) * P, base:base + TT],
                       xhT_h[:, g * TT:(g + 1) * TT])
                st_dma(xhatT_l_s[g * P:(g + 1) * P, base:base + TT],
                       xhT_l[:, g * TT:(g + 1) * TT])
            k.wait(te, "dve")
            for g in range(DG):
                ps = ps_a if g % 2 == 0 else ps_b
                k.wait(te, "act")
                grp = Accum(te, ps[:], 3 * DG)
                ii = None
                for kb in range(DG):
                    ii = grp.mm3(wst_h[:, kb * D + g * P:kb * D + (g + 1) * P],
                                 wst_l[:, kb * D + g * P:kb * D + (g + 1) * P],
                                 xhT_h[:, kb * TT:(kb + 1) * TT],
                                 xhT_l[:, kb * TT:(kb + 1) * TT])
                k.inc(ii, "te")
                k.wait(act, "te")
                act.activation(e1[:, 0:TT], ps[:], AF.Sigmoid, bias=ba_t[:, g:g + 1], scale=1.0)
                k.inc(act.activation(e2[:, 0:TT], e1[:, 0:TT], AF.Ln,
                                     bias=eps6[:, 0:1], scale=1.0), "act")
                k.wait(dve, "act")
                dve.memset(e3[:, 0:1], 0.0)
                k.inc(dve.tensor_tensor_scan(out=e3[:, 1:TT + 1], data0=e2[:, 0:TT],
                                             data1=e2[:, 0:TT], initial=0.0,
                                             op0=ALU.add, op1=ALU.bypass), "dve")
                k.wait(act, "dve")
                k.inc(act.copy(stats[:, 5:6], e3[:, TT:TT + 1]), "act")
                k.wait(dve, "act")
                k.inc(dve.tensor_add(out=Gc[:, g:g + 1], in0=Gc[:, g:g + 1],
                                     in1=stats[:, 5:6]), "dve")
                k.wait(act, "dve")
                act.wait_ge(GQ, k.cnt["gq"])
                k.inc(act.activation(gq_f[:, 0:TT], e3[:, 0:TT], AF.Exp,
                                     bias=Gc[:, g:g + 1], scale=-1.0), "act")
                k.wait(gp, "act")
                st_dma(g_s[g * P:(g + 1) * P, base:base + TT], gq_f[:, 0:TT])

        bq_epoch(4)
        k.wait_all(sp)
        sp.wait_ge(GQ, k.cnt["gq"])
        for ti in range(NT):
            base = ti * TT
            k.wait_all(sp)
            sp.wait_ge(GQ, k.cnt["gq"])
            for j in range(NSUB):
                ld_dma(xt[:, j * D:(j + 1) * D], x_d[base + j * P:base + (j + 1) * P, :])
            for g in range(DG):
                ld_dma(xhT_h[:, g * TT:(g + 1) * TT],
                       xhatT_h_s[g * P:(g + 1) * P, base:base + TT])
                ld_dma(xhT_l[:, g * TT:(g + 1) * TT],
                       xhatT_l_s[g * P:(g + 1) * P, base:base + TT])

            def proj(wname, out_h, out_l, bias_t, extra=None):
                k.wait(sp, "te")
                for kb in range(DG):
                    ld_dma(wst_h[:, kb * D:(kb + 1) * D],
                           wT_s[wname][0][kb * P:(kb + 1) * P, :])
                    ld_dma(wst_l[:, kb * D:(kb + 1) * D],
                           wT_s[wname][1][kb * P:(kb + 1) * P, :])
                k.wait(te, "ld")
                for g in range(DG):
                    ps = ps_a if g % 2 == 0 else ps_b
                    k.wait(te, "act")
                    k.wait(te, "dve")
                    grp = Accum(te, ps[:], 3 * DG)
                    ii = None
                    for kb in range(DG):
                        ii = grp.mm3(wst_h[:, kb * D + g * P:kb * D + (g + 1) * P],
                                     wst_l[:, kb * D + g * P:kb * D + (g + 1) * P],
                                     xhT_h[:, kb * TT:(kb + 1) * TT],
                                     xhT_l[:, kb * TT:(kb + 1) * TT])
                    k.inc(ii, "te")
                    k.wait(act, "te")
                    k.wait(act, "dve")
                    k.inc(act.activation(qv_f[:, 0:TT], ps[:], AF.Relu,
                                         bias=bias_t[:, g:g + 1], scale=1.0), "act")
                    k.wait(dve, "act")
                    if extra is not None:
                        extra(g)
                    emit_split(k, dve, qv_f[:, 0:TT],
                               out_h[:, g * TT:(g + 1) * TT], out_l[:, g * TT:(g + 1) * TT])

            proj("wq", q_h, q_l, bq_t)

            def k_extra(g):
                k.wait(sp, "dve")
                ld_dma(gq_f[:, 0:TT], g_s[g * P:(g + 1) * P, base:base + TT])
                k.wait(dve, "ld")
                k.inc(dve.tensor_mul(out=qv_f[:, 0:TT], in0=qv_f[:, 0:TT],
                                     in1=gq_f[:, 0:TT]), "dve")
            proj("wk", wk_h, wk_l, bk_t, extra=k_extra)

            k.wait(sp, "te")
            for kb in range(DG):
                ld_dma(wst_h[:, kb * D:(kb + 1) * D], wT_s["wv"][0][kb * P:(kb + 1) * P, :])
                ld_dma(wst_l[:, kb * D:(kb + 1) * D], wT_s["wv"][1][kb * P:(kb + 1) * P, :])
            k.wait(te, "ld")
            for c in range(NSUB):
                k.wait(te, "act")
                k.wait(te, "dve")
                ii = None
                for half, ps in ((0, ps_v0), (1, ps_v1)):
                    grp = Accum(te, ps[:], 3 * DG + 2)
                    for kb in range(DG):
                        grp.mm3(xhT_h[:, kb * TT + c * P:kb * TT + (c + 1) * P],
                                xhT_l[:, kb * TT + c * P:kb * TT + (c + 1) * P],
                                wst_h[:, kb * D + half * TT:kb * D + (half + 1) * TT],
                                wst_l[:, kb * D + half * TT:kb * D + (half + 1) * TT])
                    grp.mm(ones1[:, 0:P], bv_h[0:1, half * TT:(half + 1) * TT])
                    ii = grp.mm(ones1[:, 0:P], bv_l[0:1, half * TT:(half + 1) * TT])
                k.inc(ii, "te")
                for half, ps in ((0, ps_v0), (1, ps_v1)):
                    k.wait(act, "te")
                    k.wait(act, "dve")
                    k.inc(act.activation(qv_f[:, 0:TT], ps[:], AF.Copy, bias=0.0,
                                         scale=1.0), "act")
                    k.wait(dve, "act")
                    emit_split(k, dve, qv_f[:, 0:TT],
                               v_h[:, c * D + half * TT:c * D + (half + 1) * TT],
                               v_l[:, c * D + half * TT:c * D + (half + 1) * TT])

            if ti == 0:
                bq_epoch(5)
            for c in range(NSUB):
                for src, dstT in ((wk_h, wkT_h), (wk_l, wkT_l)):
                    k.wait(te, "dve")
                    ii = None
                    for g in range(DG):
                        ii = te.transpose(ps_t[:, g * P:(g + 1) * P],
                                          src[:, g * TT + c * P:g * TT + (c + 1) * P],
                                          ident_t[:])
                    k.inc(ii, "te")
                    k.wait(dve, "te")
                    k.inc(dve.tensor_copy(out=dstT[:, c * D:(c + 1) * D], in_=ps_t[:]), "dve")

            for c in range(NSUB):
                acc0 = Accum(te, ps_n0[:], 3 * DG + 3 * (c + 1))
                acc1 = Accum(te, ps_n1[:], 3 * DG + 3 * (c + 1))
                accd = Accum(te, ps_v0[:, 0:1], 3 * DG + 2 * (c + 1))
                k.wait(te, "act")
                k.wait(te, "dve")
                for g in range(DG):
                    lh = q_h[:, g * TT + c * P:g * TT + (c + 1) * P]
                    llo = q_l[:, g * TT + c * P:g * TT + (c + 1) * P]
                    acc0.mm3(lh, llo, S_h[:, g * D:g * D + TT], S_l[:, g * D:g * D + TT])
                    acc1.mm3(lh, llo, S_h[:, g * D + TT:(g + 1) * D],
                             S_l[:, g * D + TT:(g + 1) * D])
                    accd.mm3(lh, llo, z_h[:, g:g + 1], z_l[:, g:g + 1])
                for s in range(c + 1):
                    scTh = scT_h0 if s % 2 == 0 else scT_h1
                    scTl = scT_l0 if s % 2 == 0 else scT_l1
                    grp = Accum(te, ps_sc[:, 0:P], 3 * DG)
                    ii = None
                    for g in range(DG):
                        ii = grp.mm3(wk_h[:, g * TT + s * P:g * TT + (s + 1) * P],
                                     wk_l[:, g * TT + s * P:g * TT + (s + 1) * P],
                                     q_h[:, g * TT + c * P:g * TT + (c + 1) * P],
                                     q_l[:, g * TT + c * P:g * TT + (c + 1) * P])
                    k.inc(ii, "te")
                    k.wait(dve, "te")
                    if s == c:
                        k.inc(dve.tensor_mul(out=sc_f[:], in0=ps_sc[:, 0:P],
                                             in1=trimask_t[:]), "dve")
                    else:
                        k.inc(dve.tensor_copy(out=sc_f[:], in_=ps_sc[:, 0:P]), "dve")
                    emit_split(k, dve, sc_f[:], scTh[:], scTl[:])
                    k.wait(te, "dve")
                    acc0.mm3(scTh[:], scTl[:], v_h[:, s * D:s * D + TT],
                             v_l[:, s * D:s * D + TT])
                    acc1.mm3(scTh[:], scTl[:], v_h[:, s * D + TT:(s + 1) * D],
                             v_l[:, s * D + TT:(s + 1) * D])
                    accd.mm(scTh[:], onescol[:])
                    ii = accd.mm(scTl[:], onescol[:])
                    if s == c:
                        k.inc(ii, "te")
                k.wait(act, "te")
                act.activation(recd[:, 0:1], ps_v0[:, 0:1], AF.Ln,
                               bias=eps6[:, 0:1], scale=1.0)
                k.inc(act.activation(recd[:, 0:1], recd[:, 0:1], AF.Exp,
                                     bias=0.0, scale=-1.0), "act")
                k.wait(dve, "act")
                dve.scalar_tensor_tensor(out=xt[:, c * D:c * D + TT], in0=ps_n0[:],
                                         scalar=recd[:, 0:1], in1=xt[:, c * D:c * D + TT],
                                         op0=ALU.mult, op1=ALU.add)
                k.inc(dve.scalar_tensor_tensor(out=xt[:, c * D + TT:(c + 1) * D], in0=ps_n1[:],
                                               scalar=recd[:, 0:1],
                                               in1=xt[:, c * D + TT:(c + 1) * D],
                                               op0=ALU.mult, op1=ALU.add), "dve")

            if DBG:
                k.wait(gp, "dve")
                for j in range(NSUB):
                    st_dma(x2_dbg[base + j * P:base + (j + 1) * P, :],
                           xt[:, j * D:(j + 1) * D])
            if ti == 0:
                bq_epoch(6)

            for p4 in range(4):
                banks = (ps_a, ps_b, ps_n0, ps_n1)
                k.wait(te, "dve")
                ii = None
                for gi, g in enumerate((2 * p4, 2 * p4 + 1)):
                    for half in range(2):
                        ps = banks[gi * 2 + half]
                        grp = Accum(te, ps[:], 3 * NSUB)
                        for cc in range(NSUB):
                            ii = grp.mm3(
                                wkT_h[:, cc * D + g * P:cc * D + (g + 1) * P],
                                wkT_l[:, cc * D + g * P:cc * D + (g + 1) * P],
                                v_h[:, cc * D + half * TT:cc * D + (half + 1) * TT],
                                v_l[:, cc * D + half * TT:cc * D + (half + 1) * TT])
                k.inc(ii, "te")
                k.wait(dve, "te")
                for gi, g in enumerate((2 * p4, 2 * p4 + 1)):
                    for half in range(2):
                        ps = banks[gi * 2 + half]
                        slh = S_h[:, g * D + half * TT:g * D + (half + 1) * TT]
                        sll = S_l[:, g * D + half * TT:g * D + (half + 1) * TT]
                        dve.tensor_add(out=qv_f[:, 0:TT], in0=ps[:], in1=slh)
                        dve.tensor_add(out=qv_f[:, 0:TT], in0=qv_f[:, 0:TT], in1=sll)
                        emit_split(k, dve, qv_f[:, 0:TT], slh, sll)
            for g in range(DG):
                k.wait(dve, "act")
                dve.reduce_sum(out=stats[:, 8:9], in_=wk_h[:, g * TT:(g + 1) * TT], axis=AX)
                k.inc(dve.reduce_sum(out=stats[:, 9:10], in_=wk_l[:, g * TT:(g + 1) * TT],
                                     axis=AX), "dve")
                k.wait(act, "dve")
                k.inc(act.add(stats[:, 10:11], stats[:, 8:9], stats[:, 9:10]), "act")
                k.wait(dve, "act")
                k.inc(dve.tensor_add(out=z_f[:, g:g + 1], in0=z_f[:, g:g + 1],
                                     in1=stats[:, 10:11]), "dve")
                k.wait(act, "dve")
                k.inc(act.copy(z_h[:, g:g + 1], z_f[:, g:g + 1]), "act")
                k.wait(dve, "act")
                k.inc(dve.tensor_tensor(out=z_l[:, g:g + 1], in0=z_f[:, g:g + 1],
                                        in1=z_h[:, g:g + 1], op=ALU.subtract), "dve")

            if ti == 0:
                bq_epoch(7)
            for ht in range(2):
                jbase = ht * 2
                for jj in range(2):
                    j = jbase + jj
                    emit_ln(k, act, dve, xt[:, j * D:(j + 1) * D], xh_f[:], stats, eps5,
                            xh_f[:])
                    for hhh in range(2):
                        sl = slice(hhh * TT, (hhh + 1) * TT)
                        emit_fq_signed(k, act, dve, xh_f[:, sl], xh_h[:, sl],
                                       f1[:, 0:TT], e2[:, 0:TT], j1[:, 0:TT], j2[:, 0:TT],
                                       f2[:, 0:TT], f3[:, 0:TT], f4[:, 0:TT])
                    k.wait(te, "dve")
                    k.wait(te, "act")
                    ii = None
                    for g in range(DG):
                        ii = te.transpose(ps_t[:, g * P:(g + 1) * P],
                                          xh_h[:, g * P:(g + 1) * P], ident_t[:])
                    k.inc(ii, "te")
                    k.wait(dve, "te")
                    k.inc(dve.tensor_copy(
                        out=x2qT[:, :].rearrange("p (g t) -> p g t", g=DG)
                        [:, :, jj * P:(jj + 1) * P],
                        in_=ps_t[:, :].rearrange("p (g t) -> p g t", g=DG)), "dve")
                if DBG:
                    k.wait(gp, "dve")
                    st_dma(x2q_dbg[:, (ti * 2 + ht) * DG * HT:
                                   (ti * 2 + ht + 1) * DG * HT], x2qT[:, :])
                    k.wait(dve, "gq")
                for ng2 in range(HG // 2):
                    k.wait(sp, "te")
                    ld_dma(w1blk[:, 0:D], w1qT_s[:, (2 * ng2) * D:(2 * ng2 + 1) * D])
                    ld_dma(w1blk2[:, 0:D], w1qT_s[:, (2 * ng2 + 1) * D:(2 * ng2 + 2) * D])
                    k.wait(te, "ld")
                    k.wait(te, "act")
                    k.wait(te, "dve")
                    if DBG:
                        k.wait(gp, "ld")
                        st_dma(w1b_dbg[:, ((ti * 2 + ht) * HG + 2 * ng2) * D:
                                       ((ti * 2 + ht) * HG + 2 * ng2 + 2) * D],
                               AR[:, DG * HT:DG * HT + 2 * D])
                        k.wait(sp, "gq")
                    for gi, src in ((0, w1blk), (1, w1blk2)):
                        ps = ps_a if gi == 0 else ps_b
                        grp = Accum(te, ps[:, 0:HT], DG)
                        ii = None
                        for kb in range(DG):
                            ii = grp.mm(src[:, kb * P:(kb + 1) * P],
                                        x2qT[:, kb * HT:(kb + 1) * HT])
                        k.inc(ii, "te")
                    if DBG and ti == 0 and ht == 0:
                        k.wait(dve, "te")
                        dve.tensor_copy(out=f6[:, 0:HT], in_=ps_a[:, 0:HT])
                        k.inc(dve.tensor_copy(out=f6[:, HT:2 * HT],
                                              in_=ps_b[:, 0:HT]), "dve")
                        k.wait(gp, "dve")
                        st_dma(ps_dbg[:, 2 * ng2 * HT:(2 * ng2 + 2) * HT],
                               f6[:, 0:2 * HT])
                        k.wait(dve, "gq")
                    k.wait(act, "te")
                    k.wait(act, "dve")
                    act.activation(f1[:, 0:HT], ps_a[:, 0:HT], AF.Relu,
                                   bias=b1q[:, 2 * ng2:2 * ng2 + 1], scale=1.0)
                    k.inc(act.activation(f1[:, HT:2 * HT], ps_b[:, 0:HT], AF.Relu,
                                         bias=b1q[:, 2 * ng2 + 1:2 * ng2 + 2], scale=1.0),
                          "act")
                    if DBG:
                        k.wait(gp, "act")
                        st_dma(f1_dbg[:, ((ti * 2 + ht) * HG + 2 * ng2) * HT:
                                      ((ti * 2 + ht) * HG + 2 * ng2 + 2) * HT],
                               f1[:, 0:FS])
                        k.wait(act, "gq")
                        k.wait(dve, "gq")
                    k.wait(dve, "act")
                    emit_hchain(k, dve, f1[:, 0:FS],
                                hq_bf[:, (2 * ng2) * HT:(2 * ng2 + 2) * HT],
                                j1[:, 0:FS], j2[:, 0:FS], f2[:, 0:FS], f3[:, 0:FS],
                                f4[:, 0:FS])
                if DBG:
                    k.wait(gp, "dve")
                    st_dma(hq_dbg[:, (ti * 2 + ht) * HG * HT:
                                  (ti * 2 + ht + 1) * HG * HT], hq_bf[:])
                    st_dma(x2q2_dbg[:, (ti * 2 + ht) * DG * HT:
                                    (ti * 2 + ht + 1) * DG * HT], x2qT[:, :])
                    if ti == 0 and ht == 0:
                        st_dma(bq_dbg[:, 0:HG], b1q[:])
                        st_dma(bq_dbg[:, HG:HG + DG], b2q[:])
                    k.wait(dve, "gq")
                for mg in range(DG):
                    ps = ps_a if mg % 2 == 0 else ps_b
                    k.wait(te, "act")
                    k.wait(te, "dve")
                    k.wait(sp, "te")
                    ld_dma(w2blk[:, :], w2qT_s[:, mg * HG * P:(mg + 1) * HG * P])
                    k.wait(te, "ld")
                    grp = Accum(te, ps[:, 0:HT], HG)
                    ii = None
                    for ng in range(HG):
                        ii = grp.mm(w2blk[:, ng * P:(ng + 1) * P],
                                    hq_bf[:, ng * HT:(ng + 1) * HT])
                    k.inc(ii, "te")
                    k.wait(dve, "te")
                    k.inc(dve.tensor_scalar(out=f5[:, (mg % 2) * HT:(mg % 2 + 1) * HT],
                                            in0=ps[:, 0:HT], scalar1=b2q[:, mg:mg + 1],
                                            scalar2=None, op0=ALU.add), "dve")
                    if mg % 2 == 1:
                        emit_rm_signed(
                            k, act, dve, f5[:, 0:FS], f5[:, 0:FS],
                            f2[:, 0:FS], f3[:, 0:FS], j1[:, 0:FS], j2[:, 0:FS],
                            f4[:, 0:FS], f6[:, 0:FS], f1[:, 0:FS], qv_f[:, 0:FS])
                        dve.tensor_copy(
                            out=y2_bf[:, (mg - 1) * TT + ht * HT:(mg - 1) * TT + (ht + 1) * HT],
                            in_=f5[:, 0:HT])
                        k.inc(dve.tensor_copy(
                            out=y2_bf[:, mg * TT + ht * HT:mg * TT + (ht + 1) * HT],
                            in_=f5[:, HT:2 * HT]), "dve")

            if DBG:
                k.wait(gp, "dve")
                st_dma(y2_dbg[:, ti * DG * TT:(ti + 1) * DG * TT], y2_bf[:])

            for j in range(NSUB):
                k.wait(sp, "dve")
                ld_dma(mask_t[:], mask_d[base + j * P:base + (j + 1) * P, :])
                k.wait(te, "dve")
                ii = None
                for g in range(DG):
                    ii = te.transpose(ps_t[:, g * P:(g + 1) * P],
                                      y2_bf[:, g * TT + j * P:g * TT + (j + 1) * P],
                                      ident_t[:])
                k.inc(ii, "te")
                k.wait(dve, "te")
                k.wait(dve, "ld")
                dve.wait_ge(GQ, k.cnt["gq"])
                dve.tensor_mul(out=out_t[:], in0=ps_t[:], in1=mask_t[:])
                k.inc(dve.tensor_add(out=out_t[:], in0=out_t[:],
                                     in1=xt[:, j * D:(j + 1) * D]), "dve")
                k.wait(gp, "dve")
                st_dma(out_d[base + j * P:base + (j + 1) * P, :], out_t[:])

        gp.wait_ge(GQ, k.cnt["gq"])
    return nc


_PROG_CACHE = {}


def _get_prog(T):
    if T not in _PROG_CACHE:
        _PROG_CACHE[T] = build_program(T)
    return _PROG_CACHE[T]


def _tf_rounds(x0, x1, rots):
    for r in rots:
        x0 = (x0 + x1).astype(np.uint32)
        x1 = ((x1 << np.uint32(r)) | (x1 >> np.uint32(32 - r))).astype(np.uint32)
        x1 = x0 ^ x1
    return x0, x1


def _threefry2x32(k1, k2, x0, x1):
    r0 = (13, 15, 26, 6)
    r1 = (17, 29, 16, 24)
    ks0 = np.uint32(k1)
    ks1 = np.uint32(k2)
    ks2 = np.uint32(ks0 ^ ks1 ^ np.uint32(0x1BD11BDA))
    x0 = (x0 + ks0).astype(np.uint32)
    x1 = (x1 + ks1).astype(np.uint32)
    for rots, ka, kb, i in ((r0, ks1, ks2, 1), (r1, ks2, ks0, 2), (r0, ks0, ks1, 3),
                            (r1, ks1, ks2, 4), (r0, ks2, ks0, 5)):
        x0, x1 = _tf_rounds(x0, x1, rots)
        x0 = (x0 + ka).astype(np.uint32)
        x1 = (x1 + kb + np.uint32(i)).astype(np.uint32)
    return x0, x1


def _consts():
    ident = np.eye(P, dtype=ml_dtypes.bfloat16)
    tri = np.triu(np.ones((P, P), dtype=np.float32))
    n = 4 * 8192 * D
    i = np.arange(n, dtype=np.uint64)
    b1, b2 = _threefry2x32(0, 42, (i >> np.uint64(32)).astype(np.uint32),
                           i.astype(np.uint32))
    bits = b1 ^ b2
    u = ((bits >> np.uint32(9)) | np.uint32(0x3F800000)).view(np.float32) - np.float32(1.0)
    keep = (u < np.float32(0.8)).reshape(4, 8192, D)
    maskv = np.where(keep, np.float32(1.25), np.float32(0.0)).astype(ml_dtypes.bfloat16)
    return ident, tri, maskv


def kernel(L0_attn_x_seq, L0_attn_W_q, L0_attn_b_q, L0_attn_W_k, L0_attn_b_k,
           L0_attn_W_v, L0_attn_b_v, L0_attn_W_a, L0_attn_b_a,
           L0_ffn_W1, L0_ffn_b1, L0_ffn_W2, L0_ffn_b2):
    B, T, _ = L0_attn_x_seq.shape
    ident, tri, maskv = _consts()
    nc = _get_prog(T)

    def bvec(a, gg):
        return np.ascontiguousarray(np.asarray(a).reshape(gg, P).T)
    base = {
        "wq": np.ascontiguousarray(L0_attn_W_q), "wk": np.ascontiguousarray(L0_attn_W_k),
        "wv": np.ascontiguousarray(L0_attn_W_v), "wa": np.ascontiguousarray(L0_attn_W_a),
        "bq": bvec(L0_attn_b_q, DG), "bk": bvec(L0_attn_b_k, DG),
        "bv": np.ascontiguousarray(np.asarray(L0_attn_b_v)[None, :]),
        "ba": bvec(L0_attn_b_a, DG),
        "w1": np.ascontiguousarray(L0_ffn_W1), "b1": bvec(L0_ffn_b1, HG),
        "w2": np.ascontiguousarray(L0_ffn_W2), "b2": bvec(L0_ffn_b2, DG),
        "ident": ident, "trimask": tri,
    }
    in_maps = []
    for core in range(8):
        b = core % B
        m = dict(base)
        m["x"] = np.ascontiguousarray(L0_attn_x_seq[b])
        m["mask"] = np.ascontiguousarray(maskv[b, :T, :])
        in_maps.append(m)
    res = run_bass_kernel_spmd(nc, in_maps, core_ids=list(range(8)))
    out = np.stack([res.results[b]["out"] for b in range(B)], axis=0)
    return out
